# version 07
# baseline (speedup 1.0000x reference)
import numpy as np
import ml_dtypes
import concourse.bass as bass
import concourse.mybir as mybir
from concourse.bass_utils import run_bass_kernel_spmd

F32 = mybir.dt.float32
BF16 = mybir.dt.bfloat16
AF = mybir.ActivationFunctionType
ALU = mybir.AluOpType

D = 2048
S = 2048
TOWN = 1024
NQ, NKV, HD = 32, 8, 64
H = 8192
NC_CHUNKS = D // 128
SHIFT = -20.0

ENGINES = ("t", "a", "v", "g", "s")


SEMS = ("t", "a", "v", "g0", "g1", "g2", "g3", "s0", "s1", "s2", "s3")


class Sched:

    def __init__(self, nc):
        self.nc = nc
        self.sems = {}
        self.cnt = {s: 0 for s in SEMS}
        self.maxw = {}
        self.dma_idx = {"g": 0, "s": 0}
        self.last_dma_tok = {"g": {}, "s": {}}

    def open(self):
        for s in SEMS:
            self.sems[s] = self.nc.alloc_semaphore(f"sem_{s}")

    def begin_phase(self):
        self.ops = {e: [] for e in ENGINES}

    def add(self, e, fn, waits=(), inc=False, inc_n=1):
        if e in ("g", "s") and inc:
            semkey = e + str(self.dma_idx[e] % 4)
            self.dma_idx[e] += 1
        else:
            semkey = e
        w = []
        for t in waits:
            if not t:
                continue
            sk, v = t
            if v <= 0:
                continue
            if sk == e and e in ("t", "a", "v"):
                continue
            w.append((sk, v))
        self.ops[e].append((w, fn, inc, inc_n, semkey))
        if inc:
            self.cnt[semkey] += inc_n
            if e in ("g", "s"):
                self.last_dma_tok[e][semkey] = (semkey, self.cnt[semkey])
        return (semkey, self.cnt.get(semkey, 0))

    def dma_tokens(self, e):
        return list(self.last_dma_tok[e].values())

    def run(self, block):
        def runner(e):
            def go(eng):
                prev_was_dma = [False]
                for waits, fn, inc, inc_n, semkey in self.ops[e]:
                    emitted = False
                    for (sk, v) in waits:
                        key = (e, sk)
                        if self.maxw.get(key, 0) < v:
                            eng.wait_ge(self.sems[sk], v)
                            self.maxw[key] = v
                            emitted = True
                    is_dma = inc and e in ("g", "s")
                    if is_dma and prev_was_dma[0] and not emitted:
                        eng.wait_ge(self.sems[semkey],
                                    self.maxw.get((e, semkey), 0))
                    ins = fn(eng)
                    if inc:
                        ins.then_inc(self.sems[semkey], inc_n)
                    prev_was_dma[0] = is_dma
            return go

        block.tensor(runner("t"))
        block.scalar(runner("a"))
        block.vector(runner("v"))
        block.gpsimd(runner("g"))
        block.sync(runner("s"))


def build_program(phases=6, taps=(), p3_units=64):
    nc = bass.Bass("TRN2", target_bir_lowering=False)

    xfull = nc.dram_tensor("x", [D, S], F32, kind="ExternalInput").ap()
    wqkv = nc.dram_tensor("wqkv", [D, 3072], F32, kind="ExternalInput").ap()
    bqkv = nc.dram_tensor("bqkv", [3072], F32, kind="ExternalInput").ap()
    wo = nc.dram_tensor("wo", [D, D], F32, kind="ExternalInput").ap()
    bo = nc.dram_tensor("bo", [D], F32, kind="ExternalInput").ap()
    upt = nc.dram_tensor("upt", [D, H], F32, kind="ExternalInput").ap()
    dnt = nc.dram_tensor("dnt", [H, D], F32, kind="ExternalInput").ap()
    us = nc.dram_tensor("us", [H], F32, kind="ExternalInput").ap()
    ds = nc.dram_tensor("ds", [D], F32, kind="ExternalInput").ap()
    osc = nc.dram_tensor("osc", [1], F32, kind="ExternalInput").ap()
    out = nc.dram_tensor("out", [D, TOWN], F32, kind="ExternalOutput").ap()

    tap_ts = {}
    for tname, shape, dt in (
        ("tap_xn", [128, 16 * TOWN], F32), ("tap_q", [128, 16 * TOWN], F32),
        ("tap_k", [128, 4 * S], F32), ("tap_vaug", [128, 16 * 520], F32),
        ("tap_ao", [128, 16 * TOWN], F32), ("tap_s", [128, 16 * TOWN], F32),
        ("tap_exp", [128, 8192], F32),
        ("tap_g", [128, TOWN], F32),
    ):
        if tname in taps:
            tap_ts[tname] = nc.dram_tensor(tname, shape, dt, kind="ExternalOutput").ap()

    xfull_r = xfull.rearrange("(c p) t -> p c t", p=128)
    wqkv_r = wqkv.rearrange("(c p) f -> p c f", p=128)
    wo_r = wo.rearrange("(c p) e -> p c e", p=128)
    upt_r = upt.rearrange("(c p) h -> p c h", p=128)
    dnt_r = dnt.rearrange("(c p) f -> p c f", p=128)

    sch = Sched(nc)
    sch.open()

    def tap_block(name, src_ap):
        if name not in tap_ts:
            return
        dst = tap_ts[name]
        with nc.Block(f"blk_{name}") as blk:
            def go(eng, dst=dst, src=src_ap):
                sk = "g" + str(sch.dma_idx["g"] % 4)
                sch.dma_idx["g"] += 1
                sch.cnt[sk] += 16
                sch.last_dma_tok["g"][sk] = (sk, sch.cnt[sk])
                eng.dma_start(out=dst, in_=src).then_inc(sch.sems[sk], 16)
                eng.wait_ge(sch.sems[sk], sch.cnt[sk])
            blk.gpsimd(go)

    from contextlib import ExitStack
    with ExitStack() as _ctx:
        _e = _ctx.enter_context
        ones_bf = _e(nc.sbuf_tensor("ones_bf", [128, 1], BF16))
        onesf = _e(nc.sbuf_tensor("onesf", [128, 128], F32))
        c0125 = _e(nc.sbuf_tensor("c0125", [128, 1], F32))
        cn20 = _e(nc.sbuf_tensor("cn20", [128, 1], F32))
        ceps = _e(nc.sbuf_tensor("ceps", [128, 1], F32))
        cinvn = _e(nc.sbuf_tensor("cinvn", [128, 1], F32))
        bq_sb = _e(nc.sbuf_tensor("bq_sb", [128, 24], F32))
        qbias = _e(nc.sbuf_tensor("qbias", [128, 16], F32))
        bo_sb = _e(nc.sbuf_tensor("bo_sb", [128, 16], F32))
        us_sb = _e(nc.sbuf_tensor("us_sb", [128, 64], F32))
        ds_sb = _e(nc.sbuf_tensor("ds_sb", [128, 16], F32))
        dsos = _e(nc.sbuf_tensor("dsos", [128, 16], F32))
        os_sb = _e(nc.sbuf_tensor("os_sb", [1, 1], F32))
        os_col = _e(nc.sbuf_tensor("os_col", [128, 1], F32))
        vb_row = _e(nc.sbuf_tensor("vb_row", [1, 512], F32))
        vb_bc = _e(nc.sbuf_tensor("vb_bc", [128, 512], BF16))
        recip_f = _e(nc.sbuf_tensor("recip_f", [128, 512], F32))
        bc_sb = _e(nc.sbuf_tensor("bc_sb", [128, 512], BF16))
        bc_sb2 = _e(nc.sbuf_tensor("bc_sb2", [128, 512], BF16))
        ao_tmp = _e(nc.sbuf_tensor("ao_tmp", [64, 1024], BF16))
        recip_b = _e(nc.sbuf_tensor("recip_b", [128, 512], BF16))
        onesb2 = _e(nc.sbuf_tensor("onesb2", [128, 128], BF16))
        A1 = _e(nc.sbuf_tensor("A1", [128, 16 * TOWN], BF16))
        Cxn = _e(nc.sbuf_tensor("C", [128, 16 * TOWN], BF16))
        W1 = _e(nc.sbuf_tensor("W1", [128, 8192], BF16))
        W2 = _e(nc.sbuf_tensor("W2", [128, 8192], BF16))
        Hb = _e(nc.sbuf_tensor("Hb", [128, 4096], BF16))
        sch.begin_phase()
        with nc.psum_tensor("p0a", [128, 1], F32) as p0a, \
             nc.psum_tensor("p0b", [128, 512], F32) as p0b, \
             nc.Block("blk_p0") as blk:
            def f(eng):
                eng.memset(ones_bf[:, :], 1.0)
                eng.memset(onesf[:, :], 1.0)
                eng.memset(onesb2[:, :], 1.0)
                eng.memset(c0125[:, :], 0.125)
                eng.memset(cn20[:, :], SHIFT)
                eng.memset(ceps[:, :], 1e-5)
                return eng.memset(cinvn[:, :], 1.0 / D)
            ev_ms = sch.add("v", f, inc=True)

            for dst, srcfn in (
                (bq_sb, lambda: bqkv.rearrange("(a b) -> b a", b=128)),
                (bo_sb, lambda: bo.rearrange("(a b) -> b a", b=128)),
                (us_sb, lambda: us.rearrange("(a b) -> b a", b=128)),
                (ds_sb, lambda: ds.rearrange("(a b) -> b a", b=128)),
                (os_sb, lambda: osc.rearrange("(a b) -> a b", a=1)),
                (vb_row, lambda: bqkv[2560:3072].rearrange("(a b) -> a b", a=1)),
            ):
                def f(eng, dst=dst, srcfn=srcfn):
                    return eng.dma_start(out=dst[:, :], in_=srcfn(),
                                         allow_slow_non_contiguous=True)
                ev_ld = sch.add("s", f, inc=True, inc_n=16)

            def f(eng):
                eng.matmul(p0a[:, :], onesf[0:1, :], os_sb[0:1, 0:1], start=True, stop=True)
                return eng.matmul(p0b[:, :], onesf[0:1, :], vb_row[0:1, :], start=True, stop=True)
            ev_bc = sch.add("t", f, waits=[ev_ms] + sch.dma_tokens("s"), inc=True)

            def f(eng):
                eng.tensor_copy(os_col[:, :], p0a[:, :])
                return eng.tensor_copy(vb_bc[:, :], p0b[:, :])
            ev_cp = sch.add("v", f, waits=[ev_bc], inc=True)

            def f(eng):
                eng.mul(dsos[:, :], ds_sb[:, :], os_col[:, 0:1])
                return eng.mul(qbias[:, :], bq_sb[:, 0:16], c0125[:, 0:1])
            sch.add("a", f, waits=[ev_cp] + sch.dma_tokens("s"), inc=True)
            sch.run(blk)

        with nc.sbuf_tensor("Bbuf", [128, 16 * TOWN], BF16) as Bbuf:
            if phases >= 1:
                with nc.sbuf_tensor("rows", [1, 6 * 1024], F32) as rows, \
                     nc.sbuf_tensor("Ff", [128, 4096], F32) as Ff:
                    sch.begin_phase()
                    with nc.psum_tensor("p1s", [1, 1024], F32) as p1s, \
                         nc.psum_tensor("p1q", [1, 1024], F32) as p1q, \
                         nc.psum_tensor("p1b", [128, 512], F32) as p1b, \
                         nc.psum_tensor("p1c", [128, 512], F32) as p1c, \
                         nc.Block("blk_p1") as blk:
                        r_mean = rows[0:1, 0:1024]
                        r_ex2 = rows[0:1, 1024:2048]
                        r_var = rows[0:1, 2048:3072]
                        r_rstd = rows[0:1, 3072:4096]
                        r_b = rows[0:1, 4096:5120]
                        r_tmp = rows[0:1, 5120:6144]
                        ev_sqmm = {}
                        ev_aff = {}
                        ev_xl = {}
                        ev_stats_prev = None
                        for half, dstb in ((0, A1), (1, Bbuf)):
                            def f(eng, half=half, dstb=dstb):
                                return eng.dma_start(
                                    out=dstb.ap().rearrange("p (c t) -> p c t", t=TOWN),
                                    in_=xfull_r[:, :, half * TOWN:(half + 1) * TOWN])
                            ev_xl[half] = sch.add("g", f, inc=True, inc_n=16)
                        pre_toks = []
                        for sl in range(2):
                            Wb = W1 if sl % 2 == 0 else W2
                            def f(eng, sl=sl, Wb=Wb):
                                return eng.dma_start(
                                    out=Wb.ap().rearrange("p (c j) -> p c j", j=512),
                                    in_=wqkv_r[:, :, sl * 512:(sl + 1) * 512])
                            pre_toks.append(sch.add("g", f, inc=True, inc_n=16))
                        for half in range(2):
                            xsrc = A1 if half == 0 else Bbuf
                            xn_dst = Cxn if half == 0 else Bbuf
                            hb_off = half * 2048
                            ev_summ = None
                            for fc in range(16):
                                gfc = half * 16 + fc
                                def f(eng, fc=fc, xsrc=xsrc):
                                    for n in range(2):
                                        ins = eng.matmul(
                                            p1s[0:1, n * 512:(n + 1) * 512], ones_bf[:, :],
                                            xsrc[:, fc * TOWN + n * 512: fc * TOWN + (n + 1) * 512],
                                            start=(fc == 0), stop=(fc == 15))
                                    return ins
                                ev_summ = sch.add("t", f, waits=[ev_xl[half], ev_stats_prev],
                                                  inc=True)
                                def f(eng, fc=fc, xsrc=xsrc, hb_off=hb_off):
                                    return eng.activation(
                                        Hb[:, hb_off + (fc % 2) * 1024: hb_off + (fc % 2) * 1024 + 1024],
                                        xsrc[:, fc * TOWN:(fc + 1) * TOWN], AF.Square)
                                ev_sq = sch.add("a", f,
                                                waits=[ev_xl[half],
                                                       ev_sqmm.get(gfc - 2)],
                                                inc=True)
                                def f(eng, fc=fc, hb_off=hb_off):
                                    for n in range(2):
                                        ins = eng.matmul(
                                            p1q[0:1, n * 512:(n + 1) * 512], ones_bf[:, :],
                                            Hb[:, hb_off + (fc % 2) * 1024 + n * 512:
                                               hb_off + (fc % 2) * 1024 + (n + 1) * 512],
                                            start=(fc == 0), stop=(fc == 15))
                                    return ins
                                ev_sqmm[gfc] = sch.add("t", f,
                                                       waits=[ev_sq, ev_stats_prev],
                                                       inc=True)
                            def f(eng):
                                eng.tensor_scalar(r_mean, p1s[0:1, :], cinvn[0:1, 0:1], None, op0=ALU.mult)
                                eng.tensor_scalar(r_ex2, p1q[0:1, :], cinvn[0:1, 0:1], None, op0=ALU.mult)
                                eng.drain()
                                eng.tensor_tensor(r_tmp, r_mean, r_mean, op=ALU.mult)
                                eng.drain()
                                return eng.tensor_tensor(r_var, r_ex2, r_tmp, op=ALU.subtract)
                            ev_st = sch.add("v", f, waits=[ev_sqmm[half * 16 + 15]], inc=True)
                            ev_stats_prev = ev_st
                            def f(eng):
                                eng.activation(r_tmp, r_var, AF.Ln, bias=ceps[0:1, 0:1])
                                eng.drain()
                                return eng.activation(r_rstd, r_tmp, AF.Exp, scale=-0.5)
                            ev_rs = sch.add("a", f, waits=[ev_st], inc=True)
                            def f(eng):
                                return eng.scalar_tensor_tensor(
                                    r_b, r_mean, -1.0, r_rstd, op0=ALU.mult, op1=ALU.mult)
                            ev_bb = sch.add("v", f, waits=[ev_rs], inc=True)
                            ev_bcv = None
                            for n in range(2):
                                def f(eng, n=n):
                                    eng.matmul(p1b[:, :], onesf[0:1, :],
                                               r_rstd[0:1, n * 512:(n + 1) * 512],
                                               start=True, stop=True)
                                    return eng.matmul(p1c[:, :], onesf[0:1, :],
                                                      r_b[0:1, n * 512:(n + 1) * 512],
                                                      start=True, stop=True)
                                ev_bct = sch.add("t", f, waits=[ev_bb, ev_bcv], inc=True)
                                def f(eng, n=n):
                                    eng.tensor_copy(Ff[:, n * 512:(n + 1) * 512], p1b[:, :])
                                    return eng.tensor_copy(Ff[:, 1024 + n * 512:1024 + (n + 1) * 512],
                                                           p1c[:, :])
                                ev_bcv = sch.add("v", f, waits=[ev_bct], inc=True)
                            for fc in range(16):
                                gfc = half * 16 + fc
                                xbuf = Ff[:, 2048 + (fc % 2) * 1024: 2048 + (fc % 2) * 1024 + 1024]
                                def f(eng, fc=fc, half=half, xbuf=xbuf):
                                    return eng.dma_start(
                                        out=xbuf,
                                        in_=xfull[fc * 128:(fc + 1) * 128,
                                                  half * TOWN:(half + 1) * TOWN])
                                ev_xf = sch.add("s", f, waits=[ev_aff.get(gfc - 2)],
                                                inc=True, inc_n=16)
                                def f(eng, fc=fc, xbuf=xbuf, xn_dst=xn_dst):
                                    if fc == 0:
                                        eng.drain()
                                    d = xn_dst[:, fc * TOWN:(fc + 1) * TOWN]
                                    eng.tensor_tensor(d, xbuf, Ff[:, 0:1024], op=ALU.mult)
                                    eng.drain()
                                    return eng.tensor_tensor(d, d, Ff[:, 1024:2048], op=ALU.add)
                                ev_aff[gfc] = sch.add("v", f,
                                                      waits=[ev_xf, ev_bcv],
                                                      inc=True)
                        sch.run(blk)
                tap_block("tap_xn", Cxn.ap())

            with nc.sbuf_tensor("A2", [128, 4 * S + 16 * 520], BF16) as A2:
                Ksb = A2[:, 0:4 * S]
                Vaug_off = 4 * S
                if phases >= 2:
                    sch.begin_phase()
                    with nc.psum_tensor("p2a", [128, 512], F32) as p2a, \
                         nc.psum_tensor("p2b", [128, 512], F32) as p2b, \
                         nc.psum_tensor("p2c", [128, 512], F32) as p2c, \
                         nc.psum_tensor("p2d", [128, 512], F32) as p2d, \
                         nc.Block("blk_p2") as blk:
                        psq = [p2a, p2b]
                        psv = [p2c, p2d]
                        ev_vm = sch.add("v", lambda eng: eng.memset(A2[:, Vaug_off:], 1.0), inc=True)
                        ev_mm_sl = {}
                        ev_mm_first = {}
                        ev_dma_sl = {}
                        ev_ev = {}
                        qv_i = 0
                        kv_i = 0
                        for sl in range(6):
                            Wb = W1 if sl % 2 == 0 else W2
                            if sl < 2:
                                ev_dma_sl[sl] = pre_toks[sl]
                            else:
                                def f(eng, sl=sl, Wb=Wb):
                                    return eng.dma_start(
                                        out=Wb.ap().rearrange("p (c j) -> p c j", j=512),
                                        in_=wqkv_r[:, :, sl * 512:(sl + 1) * 512])
                                ev_dma_sl[sl] = sch.add("g", f,
                                                        waits=[ev_mm_sl.get(sl - 2),
                                                               ev_mm_first.get(sl - 1)],
                                                        inc=True, inc_n=16)
                            if sl < 4:
                                for fsub in range(4):
                                    qf = sl * 4 + fsub
                                    for tc in range(2):
                                        ps = psq[qv_i % 2]
                                        def f(eng, Wb=Wb, fsub=fsub, tc=tc, ps=ps):
                                            for c in range(16):
                                                ins = eng.matmul(
                                                    ps[:, :],
                                                    Wb[:, c * 512 + fsub * 128: c * 512 + fsub * 128 + 128],
                                                    Cxn[:, c * TOWN + tc * 512: c * TOWN + tc * 512 + 512],
                                                    start=(c == 0), stop=(c == 15))
                                            return ins
                                        ev_mm = sch.add("t", f,
                                                        waits=[ev_dma_sl[sl],
                                                               ev_ev.get(("q", qv_i % 2))],
                                                        inc=True)
                                        ev_mm_sl[sl] = ev_mm
                                        ev_mm_first.setdefault(sl, ev_mm)
                                        def f(eng, qf=qf, tc=tc, ps=ps):
                                            return eng.activation(
                                                A1[:, qf * TOWN + tc * 512: qf * TOWN + tc * 512 + 512],
                                                ps[:, :], AF.Identity,
                                                bias=qbias[:, qf:qf + 1], scale=0.125)
                                        ev_ev[("q", qv_i % 2)] = sch.add("a", f, waits=[ev_mm], inc=True)
                                        qv_i += 1
                            elif sl == 4:
                                for fsub in range(4):
                                    for tcf in range(4):
                                        ps = psq[qv_i % 2]
                                        src = Cxn if tcf < 2 else Bbuf
                                        tcl = tcf % 2
                                        def f(eng, Wb=Wb, fsub=fsub, src=src, tcl=tcl, ps=ps):
                                            for c in range(16):
                                                ins = eng.matmul(
                                                    ps[:, :],
                                                    Wb[:, c * 512 + fsub * 128: c * 512 + fsub * 128 + 128],
                                                    src[:, c * TOWN + tcl * 512: c * TOWN + tcl * 512 + 512],
                                                    start=(c == 0), stop=(c == 15))
                                            return ins
                                        ev_mm = sch.add("t", f,
                                                        waits=[ev_dma_sl[sl],
                                                               ev_ev.get(("q", qv_i % 2))],
                                                        inc=True)
                                        ev_mm_sl[sl] = ev_mm
                                        ev_mm_first.setdefault(sl, ev_mm)
                                        def f(eng, fsub=fsub, tcf=tcf, ps=ps):
                                            return eng.activation(
                                                Ksb[:, fsub * S + tcf * 512: fsub * S + tcf * 512 + 512],
                                                ps[:, :], AF.Identity,
                                                bias=bq_sb[:, 16 + fsub:17 + fsub], scale=1.0)
                                        ev_ev[("q", qv_i % 2)] = sch.add("a", f, waits=[ev_mm], inc=True)
                                        qv_i += 1
                            else:
                                for tt in range(16):
                                    ps = psv[kv_i % 2]
                                    src = Cxn if tt < 8 else Bbuf
                                    ttl = tt % 8
                                    def f(eng, Wb=Wb, src=src, ttl=ttl, ps=ps):
                                        for c in range(16):
                                            ins = eng.matmul(
                                                ps[:, :],
                                                src[:, c * TOWN + ttl * 128: c * TOWN + ttl * 128 + 128],
                                                Wb[:, c * 512: c * 512 + 512],
                                                start=(c == 0), stop=(c == 15))
                                        return ins
                                    ev_mm = sch.add("t", f,
                                                    waits=[ev_dma_sl[sl],
                                                           ev_ev.get(("v", kv_i % 2)),
                                                           ev_vm],
                                                    inc=True)
                                    ev_mm_sl[sl] = ev_mm
                                    ev_mm_first.setdefault(sl, ev_mm)
                                    def f(eng, tt=tt, ps=ps):
                                        o3 = A2[:, Vaug_off + tt * 520: Vaug_off + tt * 520 + 520]
                                        return eng.tensor_tensor(
                                            o3.rearrange("p (g e) -> p g e", e=65)[:, :, 0:64],
                                            ps[:, :].rearrange("p (g e) -> p g e", e=64),
                                            vb_bc[:, :].rearrange("p (g e) -> p g e", e=64),
                                            op=ALU.add)
                                    ev_ev[("v", kv_i % 2)] = sch.add("v", f, waits=[ev_mm], inc=True)
                                    kv_i += 1
                        sch.run(blk)
                    tap_block("tap_q", A1.ap())
                    tap_block("tap_k", Ksb)
                    tap_block("tap_vaug", A2[:, Vaug_off:])

                if phases >= 3:
                    sch.begin_phase()
                    with nc.psum_tensor("p3r0", [128, 1024], F32) as p3r0, \
                         nc.psum_tensor("p3r1", [128, 1024], F32) as p3r1, \
                         nc.psum_tensor("p3p0", [128, 512], F32) as p3p0, \
                         nc.psum_tensor("p3p1", [128, 512], F32) as p3p1, \
                         nc.psum_tensor("p3p2", [128, 512], F32) as p3p2, \
                         nc.psum_tensor("p3bc", [64, 512], F32) as p3bc, \
                         nc.Block("blk_p3") as blk:
                        regions = [p3r0, p3r1]
                        pvs = [p3p0, p3p1, p3p2]
                        expb = [W1, W2]
                        NU = p3_units
                        ev_exp_pair = {}
                        ev_exp_unit = {}
                        ev_pv = {}
                        ev_qk_pair = {}
                        ev_recip = {}
                        ev_mult = {}
                        ev_bct = {}
                        ev_aodma = {}
                        ev_bcc = {}

                        def unit_params(u):
                            h = u // 2
                            tc = u % 2
                            g = h // 4
                            c = g // 2
                            i = h % 4
                            b = (g % 2) * 64
                            return h, tc, g, c, i, b

                        for u in range(NU + 2):
                            if u < NU:
                                h, tc, g, c, i, b = unit_params(u)
                                qch = 4 * c + i
                                for j in range(8):
                                    p = 8 * u + j
                                    reg = regions[p % 2]
                                    def f(eng, j=j, reg=reg, b=b, c=c, qch=qch, tc=tc):
                                        for half in range(2):
                                            kt = 2 * j + half
                                            tp = (b, 0) if b else None
                                            ins = eng.matmul(
                                                reg[:, half * 512:(half + 1) * 512],
                                                Ksb[b:b + 64, c * S + kt * 128: c * S + kt * 128 + 128],
                                                A1[b:b + 64, qch * TOWN + tc * 512: qch * TOWN + tc * 512 + 512],
                                                start=True, stop=True, tile_position=tp)
                                        return ins
                                    ev_qk_pair[p] = sch.add(
                                        "t", f, waits=[ev_exp_pair.get(p - 2)], inc=True)
                                    def f(eng, u=u, j=j, reg=reg):
                                        return eng.activation(
                                            expb[u % 2][:, j * 1024:(j + 1) * 1024],
                                            reg[:, :], AF.Exp,
                                            bias=cn20[:, 0:1], scale=1.0)
                                    ev_exp_pair[p] = sch.add(
                                        "a", f, waits=[ev_qk_pair[p],
                                                       ev_pv.get(u - 2)], inc=True)
                                ev_exp_unit[u] = ev_exp_pair[8 * u + 7]
                            if u >= 1 and u - 1 < NU:
                                v_ = u - 1
                                hv, tcv, gv, cv, iv, bv = unit_params(v_)
                                def f(eng, v_=v_, gv=gv):
                                    eb = expb[v_ % 2]
                                    ps = pvs[v_ % 3]
                                    for kt in range(16):
                                        vbase = Vaug_off + kt * 520 + gv * 65
                                        ins = eng.matmul(
                                            ps[0:65, :],
                                            A2[:, vbase: vbase + 65],
                                            eb[:, kt * 512:(kt + 1) * 512],
                                            start=(kt == 0), stop=(kt == 15))
                                    return ins
                                ev_pv[v_] = sch.add(
                                    "t", f, waits=[ev_exp_unit.get(v_),
                                                   ev_mult.get(v_ - 3)], inc=True)
                            if u >= 2 and u - 2 < NU:
                                w_ = u - 2
                                hw, tcw, gw, cw, iw, bw = unit_params(w_)
                                def f(eng, w_=w_):
                                    return eng.matmul(p3bc[:, :],
                                                      onesb2[64:65, 0:64],
                                                      recip_b[64:65, :],
                                                      start=True, stop=True, tile_position=(64, 0))
                                ev_bct[w_] = sch.add(
                                    "t", f, waits=[ev_recip.get(w_),
                                                   ev_bcc.get(w_ - 1)], inc=True)

                            if u >= 1 and u - 1 < NU:
                                v_ = u - 1
                                def f(eng, v_=v_):
                                    eng.activation(recip_f[64:65, :],
                                                   pvs[v_ % 3][64:65, :], AF.Ln)
                                    eng.drain()
                                    return eng.activation(recip_b[64:65, :],
                                                          recip_f[64:65, :], AF.Exp,
                                                          scale=-1.0)
                                ev_recip[v_] = sch.add(
                                    "a", f, waits=[ev_pv[v_],
                                                   ev_bct.get(v_ - 1)], inc=True)
                            if u >= 2 and u - 2 < NU:
                                w_ = u - 2
                                hw, tcw, gw, cw, iw, bw = unit_params(w_)
                                qchw = 4 * cw + iw
                                if bw == 0:
                                    dst = Bbuf[0:64, qchw * TOWN + tcw * 512: qchw * TOWN + tcw * 512 + 512]
                                else:
                                    slot = w_ % 2
                                    dst = ao_tmp[0:64, slot * 512:(slot + 1) * 512]
                                def f(eng, w_=w_, dst=dst):
                                    bcb = [bc_sb, bc_sb2][w_ % 2]
                                    eng.tensor_copy(bcb[0:64, :], p3bc[:, :])
                                    eng.drain()
                                    return eng.tensor_tensor(
                                        dst, pvs[w_ % 3][0:64, :], bcb[0:64, :], op=ALU.mult)
                                ev_mult[w_] = sch.add(
                                    "v", f, waits=[ev_bct[w_],
                                                   ev_aodma.get(w_ - 2)], inc=True)
                                ev_bcc[w_] = ev_mult[w_]
                                if bw == 64:
                                    slot = w_ % 2
                                    def f(eng, w_=w_, qchw=qchw, tcw=tcw, slot=slot):
                                        return eng.dma_start(
                                            out=Bbuf[64:128, qchw * TOWN + tcw * 512: qchw * TOWN + tcw * 512 + 512],
                                            in_=ao_tmp[0:64, slot * 512:(slot + 1) * 512])
                                    ev_aodma[w_] = sch.add("s", f, waits=[ev_mult[w_]],
                                                           inc=True, inc_n=16)
                        def f(eng):
                            return eng.nop()
                        sch.add("s", f, waits=sch.dma_tokens("s"))
                        sch.run(blk)
                    tap_block("tap_exp", W1.ap())
                    tap_block("tap_ao", Bbuf.ap())

            if phases >= 4:
                sch.begin_phase()
                with nc.psum_tensor("p4a", [128, 512], F32) as p4a, \
                     nc.psum_tensor("p4b", [128, 512], F32) as p4b, \
                     nc.Block("blk_p4") as blk:
                    psq = [p4a, p4b]
                    ev_mm_sl = {}
                    ev_mm_first = {}
                    ev_ev = {}
                    qv_i = 0
                    for sl in range(4):
                        Wb = W1 if sl % 2 == 0 else W2
                        def f(eng, sl=sl, Wb=Wb):
                            return eng.dma_start(
                                out=Wb.ap().rearrange("p (c j) -> p c j", j=512),
                                in_=wo_r[:, :, sl * 512:(sl + 1) * 512])
                        ev_dma = sch.add("g", f, waits=[ev_mm_sl.get(sl - 2),
                                                        ev_mm_first.get(sl - 1)],
                                         inc=True, inc_n=16)
                        for fsub in range(4):
                            ef = sl * 4 + fsub
                            for tc in range(2):
                                ps = psq[qv_i % 2]
                                def f(eng, Wb=Wb, fsub=fsub, tc=tc, ps=ps):
                                    for c in range(16):
                                        ins = eng.matmul(
                                            ps[:, :],
                                            Wb[:, c * 512 + fsub * 128: c * 512 + fsub * 128 + 128],
                                            Bbuf[:, c * TOWN + tc * 512: c * TOWN + tc * 512 + 512],
                                            start=(c == 0), stop=(c == 15))
                                    return ins
                                ev_mm = sch.add("t", f,
                                                waits=[ev_dma,
                                                       ev_ev.get(qv_i % 2)],
                                                inc=True)
                                ev_mm_sl[sl] = ev_mm
                                ev_mm_first.setdefault(sl, ev_mm)
                                def f(eng, ef=ef, tc=tc, ps=ps):
                                    return eng.activation(
                                        A1[:, ef * TOWN + tc * 512: ef * TOWN + tc * 512 + 512],
                                        ps[:, :], AF.Identity, bias=bo_sb[:, ef:ef + 1], scale=1.0)
                                ev_ev[qv_i % 2] = sch.add("a", f, waits=[ev_mm], inc=True)
                                qv_i += 1
                    sch.run(blk)
                tap_block("tap_s", A1.ap())

        if phases >= 5:
            with nc.sbuf_tensor("Sacc", [128, 16 * TOWN], F32) as Sacc, \
                 nc.sbuf_tensor("W3", [128, 8192], BF16) as W3, \
                 nc.sbuf_tensor("xtile", [128, 2 * TOWN], F32) as xtile, \
                 nc.sbuf_tensor("gtmp", [128, 2048], BF16) as gtmp:
                sch.begin_phase()
                with nc.psum_tensor("p5u0", [128, 1024], F32) as p5u0, \
                     nc.psum_tensor("p5u1", [128, 1024], F32) as p5u1, \
                     nc.psum_tensor("p5d0", [128, 1024], F32) as p5d0, \
                     nc.psum_tensor("p5d1", [128, 1024], F32) as p5d1, \
                     nc.Block("blk_p5") as blk:
                    pu = [p5u0, p5u1]
                    pd = [p5d0, p5d1]
                    NHB = 16
                    ev_upsign = {}
                    ev_dnsign = {}
                    ev_updma = {}
                    ev_dndma = {}
                    ev_upmm = {}
                    ev_relu = {}
                    ev_dnmm = {}
                    ev_add = {}
                    for hb in range(NHB):
                        def f(eng, hb=hb):
                            return eng.dma_start(
                                out=W1.ap().rearrange("p (c j) -> p c j", j=512),
                                in_=upt_r[:, :, hb * 512:(hb + 1) * 512])
                        ev_updma[hb] = sch.add("g", f,
                                               waits=[ev_dnsign.get(hb - 1)],
                                               inc=True, inc_n=16)
                        def f(eng):
                            return eng.sign(W2[:, :], W1[:, :])
                        ev_upsign[hb] = sch.add("a", f,
                                                waits=[ev_updma[hb],
                                                       ev_upmm.get((hb - 1, 3))],
                                                inc=True)
                        def f(eng, hb=hb):
                            return eng.dma_start(
                                out=W1.ap().rearrange("p (c j) -> p c j", j=2048),
                                in_=dnt_r[:, hb * 4:(hb + 1) * 4, :])
                        ev_dndma[hb] = sch.add("g", f, waits=[ev_upsign[hb]],
                                               inc=True, inc_n=16)
                        def f(eng):
                            return eng.sign(W3[:, :], W1[:, :])
                        ev_dnsign[hb] = sch.add("a", f,
                                                waits=[ev_dndma[hb],
                                                       ev_dnmm.get((hb - 1, 15))],
                                                inc=True)
                        for hc in range(4):
                            ps = pu[hc % 2]
                            def f(eng, hb=hb, hc=hc, ps=ps):
                                for tc in range(2):
                                    for c in range(16):
                                        ins = eng.matmul(
                                            ps[:, tc * 512:(tc + 1) * 512],
                                            W2[:, c * 512 + hc * 128: c * 512 + hc * 128 + 128],
                                            Cxn[:, c * TOWN + tc * 512: c * TOWN + tc * 512 + 512],
                                            start=(c == 0), stop=(c == 15))
                                return ins
                            ev_upmm[(hb, hc)] = sch.add(
                                "t", f, waits=[ev_upsign[hb],
                                               ev_relu.get((hb, hc - 2),
                                                     ev_relu.get((hb - 1, hc + 2)))],
                                inc=True)
                            def f(eng, hb=hb, hc=hc, ps=ps):
                                return eng.activation(
                                    Hb[:, hc * 1024:(hc + 1) * 1024], ps[:, :], AF.Relu,
                                    scale=us_sb[:, hb * 4 + hc: hb * 4 + hc + 1])
                            ev_relu[(hb, hc)] = sch.add(
                                "a", f, waits=[ev_upmm[(hb, hc)],
                                               ev_dnmm.get((hb - 1, 15))], inc=True)
                        for df in range(16):
                            ps = pd[df % 2]
                            def f(eng, df=df, ps=ps):
                                for tc in range(2):
                                    for hc in range(4):
                                        ins = eng.matmul(
                                            ps[:, tc * 512:(tc + 1) * 512],
                                            W3[:, hc * 2048 + df * 128: hc * 2048 + df * 128 + 128],
                                            Hb[:, hc * 1024 + tc * 512: hc * 1024 + tc * 512 + 512],
                                            start=(hc == 0), stop=(hc == 3))
                                return ins
                            ev_dnmm[(hb, df)] = sch.add(
                                "t", f, waits=[ev_dnsign[hb],
                                               ev_relu[(hb, 3)],
                                               ev_add.get((hb, df - 2),
                                                     ev_add.get((hb - 1, df + 14)))],
                                inc=True)
                            def f(eng, hb=hb, df=df, ps=ps):
                                if hb == 0:
                                    return eng.tensor_copy(
                                        Sacc[:, df * TOWN:(df + 1) * TOWN], ps[:, :])
                                return eng.tensor_tensor(
                                    Sacc[:, df * TOWN:(df + 1) * TOWN],
                                    Sacc[:, df * TOWN:(df + 1) * TOWN], ps[:, :], op=ALU.add)
                            ev_add[(hb, df)] = sch.add(
                                "v", f, waits=[ev_dnmm[(hb, df)]], inc=True)
                    ev_silu = {}
                    ev_sg = {}
                    ev_xl6 = {}
                    ev_add6 = {}
                    ev_st6 = {}
                    for df in range(16):
                        def f(eng, df=df):
                            return eng.activation(
                                gtmp[:, (df % 2) * 1024:(df % 2) * 1024 + 1024],
                                Sacc[:, df * TOWN:(df + 1) * TOWN], AF.Silu,
                                scale=dsos[:, df:df + 1])
                        ev_silu[df] = sch.add("a", f,
                                              waits=[ev_add[(15, df)],
                                                     ev_add6.get(df - 2)], inc=True)
                        def f(eng, df=df):
                            return eng.dma_start(
                                out=xtile[:, (df % 2) * TOWN:(df % 2) * TOWN + TOWN],
                                in_=xfull[df * 128:(df + 1) * 128, 0:TOWN])
                        ev_xl6[df] = sch.add("s", f, waits=[ev_add6.get(df - 2)],
                                             inc=True, inc_n=16)
                        def f(eng, df=df):
                            eng.tensor_tensor(
                                gtmp[:, (df % 2) * 1024:(df % 2) * 1024 + 1024],
                                A1[:, df * TOWN:(df + 1) * TOWN],
                                gtmp[:, (df % 2) * 1024:(df % 2) * 1024 + 1024],
                                op=ALU.mult)
                            eng.drain()
                            return eng.tensor_tensor(
                                Sacc[:, df * TOWN:(df + 1) * TOWN],
                                xtile[:, (df % 2) * TOWN:(df % 2) * TOWN + TOWN],
                                gtmp[:, (df % 2) * 1024:(df % 2) * 1024 + 1024],
                                op=ALU.add)
                        ev_sg[df] = ev_add6[df] = sch.add(
                            "v", f, waits=[ev_silu[df], ev_xl6[df]], inc=True)
                        def f(eng, df=df):
                            return eng.dma_start(
                                out=out[df * 128:(df + 1) * 128, :],
                                in_=Sacc[:, df * TOWN:(df + 1) * TOWN])
                        ev_st6[df] = sch.add("s", f, waits=[ev_add6[df]],
                                             inc=True, inc_n=16)
                    def f(eng):
                        return eng.nop()
                    sch.add("s", f, waits=[ev_st6[15]] + sch.dma_tokens("s"))
                    sch.run(blk)

                if "tap_g" in tap_ts:
                    pass

    return nc



def _head_perm():
    perm = np.zeros(D, dtype=np.int64)
    for c in range(4):
        for i in range(4):
            for b in range(2):
                h = 8 * c + i + 4 * b
                newbase = (4 * c + i) * 128 + b * 64
                perm[newbase:newbase + 64] = h * 64 + np.arange(64)
    return perm


def _prep_inputs(L0_attn_x, L0_attn_Wqkv, L0_attn_bqkv, L0_attn_Wo, L0_attn_bo,
                 L0_ffn_up_weight, L0_ffn_down_weight, L0_ffn_up_scale,
                 L0_ffn_down_scale, L0_ffn_output_scale):
    perm = _head_perm()
    wq = np.asarray(L0_attn_Wqkv)[:, :D][:, perm]
    wkv = np.asarray(L0_attn_Wqkv)[:, D:]
    wqkv = np.ascontiguousarray(np.concatenate([wq, wkv], axis=1), dtype=np.float32)
    bq = np.asarray(L0_attn_bqkv)[:D][perm]
    bqkv = np.ascontiguousarray(np.concatenate([bq, np.asarray(L0_attn_bqkv)[D:]]),
                                dtype=np.float32)
    wo = np.ascontiguousarray(np.asarray(L0_attn_Wo)[perm, :], dtype=np.float32)
    upt = np.ascontiguousarray(np.asarray(L0_ffn_up_weight).T, dtype=np.float32)
    dnt = np.ascontiguousarray(np.asarray(L0_ffn_down_weight).T, dtype=np.float32)
    xT = np.ascontiguousarray(np.asarray(L0_attn_x).transpose(0, 2, 1), dtype=np.float32)

    common = {
        "wqkv": wqkv, "bqkv": bqkv, "wo": wo,
        "bo": np.ascontiguousarray(np.asarray(L0_attn_bo), dtype=np.float32),
        "upt": upt, "dnt": dnt,
        "us": np.ascontiguousarray(np.asarray(L0_ffn_up_scale), dtype=np.float32),
        "ds": np.ascontiguousarray(np.asarray(L0_ffn_down_scale), dtype=np.float32),
        "osc": np.ascontiguousarray(np.asarray(L0_ffn_output_scale), dtype=np.float32),
    }
    in_maps = []
    for core in range(8):
        b, half = core // 2, core % 2
        xb = xT[b]
        if half == 0:
            xc = xb
        else:
            xc = np.concatenate([xb[:, TOWN:], xb[:, :TOWN]], axis=1)
        m = dict(common)
        m["x"] = np.ascontiguousarray(xc)
        in_maps.append(m)
    return in_maps


def kernel(**inputs):
    in_maps = _prep_inputs(**inputs)
    nc = build_program()
    res = run_bass_kernel_spmd(nc, in_maps, core_ids=list(range(8)))
    outT = np.zeros((4, D, S), dtype=np.float32)
    for core in range(8):
        b, half = core // 2, core % 2
        outT[b][:, half * TOWN:(half + 1) * TOWN] = res.results[core]["out"]
    return np.ascontiguousarray(outT.transpose(0, 2, 1))


# version 08
# speedup vs baseline: 1.1999x; 1.1999x over previous
import numpy as np
import ml_dtypes
import concourse.bass as bass
import concourse.mybir as mybir
from concourse.bass_utils import run_bass_kernel_spmd

F32 = mybir.dt.float32
BF16 = mybir.dt.bfloat16
AF = mybir.ActivationFunctionType
ALU = mybir.AluOpType

D = 2048
S = 2048
TOWN = 1024
NQ, NKV, HD = 32, 8, 64
H = 8192
NC_CHUNKS = D // 128
SHIFT = -20.0

ENGINES = ("t", "a", "v", "g", "s")


SEMS = ("t", "a", "v", "g0", "g1", "g2", "g3", "s0", "s1", "s2", "s3")


class Sched:

    def __init__(self, nc):
        self.nc = nc
        self.sems = {}
        self.cnt = {s: 0 for s in SEMS}
        self.maxw = {}
        self.dma_idx = {"g": 0, "s": 0}
        self.last_dma_tok = {"g": {}, "s": {}}

    def open(self):
        for s in SEMS:
            self.sems[s] = self.nc.alloc_semaphore(f"sem_{s}")

    def begin_phase(self):
        self.ops = {e: [] for e in ENGINES}

    def add(self, e, fn, waits=(), inc=False, inc_n=1):
        if e in ("g", "s") and inc:
            semkey = e + str(self.dma_idx[e] % 4)
            self.dma_idx[e] += 1
        else:
            semkey = e
        w = []
        for t in waits:
            if not t:
                continue
            sk, v = t
            if v <= 0:
                continue
            if sk == e and e in ("t", "a", "v"):
                continue
            w.append((sk, v))
        self.ops[e].append((w, fn, inc, inc_n, semkey))
        if inc:
            self.cnt[semkey] += inc_n
            if e in ("g", "s"):
                self.last_dma_tok[e][semkey] = (semkey, self.cnt[semkey])
        return (semkey, self.cnt.get(semkey, 0))

    def dma_tokens(self, e):
        return list(self.last_dma_tok[e].values())

    def run(self, block):
        def runner(e):
            def go(eng):
                prev_was_dma = [False]
                for waits, fn, inc, inc_n, semkey in self.ops[e]:
                    emitted = False
                    for (sk, v) in waits:
                        key = (e, sk)
                        if self.maxw.get(key, 0) < v:
                            eng.wait_ge(self.sems[sk], v)
                            self.maxw[key] = v
                            emitted = True
                    is_dma = inc and e in ("g", "s")
                    if is_dma and prev_was_dma[0] and not emitted:
                        eng.wait_ge(self.sems[semkey],
                                    self.maxw.get((e, semkey), 0))
                    ins = fn(eng)
                    if inc:
                        ins.then_inc(self.sems[semkey], inc_n)
                    prev_was_dma[0] = is_dma
            return go

        block.tensor(runner("t"))
        block.scalar(runner("a"))
        block.vector(runner("v"))
        block.gpsimd(runner("g"))
        block.sync(runner("s"))


def build_program(phases=6, taps=(), p3_units=64):
    nc = bass.Bass("TRN2", target_bir_lowering=False)

    xfull = nc.dram_tensor("x", [D, S], F32, kind="ExternalInput").ap()
    wqkv = nc.dram_tensor("wqkv", [D, 3072], F32, kind="ExternalInput").ap()
    bqkv = nc.dram_tensor("bqkv", [3072], F32, kind="ExternalInput").ap()
    wo = nc.dram_tensor("wo", [D, D], F32, kind="ExternalInput").ap()
    bo = nc.dram_tensor("bo", [D], F32, kind="ExternalInput").ap()
    upt = nc.dram_tensor("upt", [D, H], F32, kind="ExternalInput").ap()
    dnt = nc.dram_tensor("dnt", [H, D], F32, kind="ExternalInput").ap()
    us = nc.dram_tensor("us", [H], F32, kind="ExternalInput").ap()
    ds = nc.dram_tensor("ds", [D], F32, kind="ExternalInput").ap()
    osc = nc.dram_tensor("osc", [1], F32, kind="ExternalInput").ap()
    out = nc.dram_tensor("out", [D, TOWN], F32, kind="ExternalOutput").ap()

    tap_ts = {}
    for tname, shape, dt in (
        ("tap_xn", [128, 16 * TOWN], F32), ("tap_q", [128, 16 * TOWN], F32),
        ("tap_k", [128, 4 * S], F32), ("tap_vaug", [128, 16 * 520], F32),
        ("tap_ao", [128, 16 * TOWN], F32), ("tap_s", [128, 16 * TOWN], F32),
        ("tap_exp", [128, 8192], F32),
        ("tap_g", [128, TOWN], F32),
    ):
        if tname in taps:
            tap_ts[tname] = nc.dram_tensor(tname, shape, dt, kind="ExternalOutput").ap()

    xfull_r = xfull.rearrange("(c p) t -> p c t", p=128)
    wqkv_r = wqkv.rearrange("(c p) f -> p c f", p=128)
    wo_r = wo.rearrange("(c p) e -> p c e", p=128)
    upt_r = upt.rearrange("(c p) h -> p c h", p=128)
    dnt_r = dnt.rearrange("(c p) f -> p c f", p=128)

    sch = Sched(nc)
    sch.open()

    def tap_block(name, src_ap):
        if name not in tap_ts:
            return
        dst = tap_ts[name]
        with nc.Block(f"blk_{name}") as blk:
            def go(eng, dst=dst, src=src_ap):
                sk = "g" + str(sch.dma_idx["g"] % 4)
                sch.dma_idx["g"] += 1
                sch.cnt[sk] += 16
                sch.last_dma_tok["g"][sk] = (sk, sch.cnt[sk])
                eng.dma_start(out=dst, in_=src).then_inc(sch.sems[sk], 16)
                eng.wait_ge(sch.sems[sk], sch.cnt[sk])
            blk.gpsimd(go)

    from contextlib import ExitStack
    with ExitStack() as _ctx:
        _e = _ctx.enter_context
        ones_bf = _e(nc.sbuf_tensor("ones_bf", [128, 1], BF16))
        onesf = _e(nc.sbuf_tensor("onesf", [128, 128], F32))
        c0125 = _e(nc.sbuf_tensor("c0125", [128, 1], F32))
        cn20 = _e(nc.sbuf_tensor("cn20", [128, 1], F32))
        ceps = _e(nc.sbuf_tensor("ceps", [128, 1], F32))
        cinvn = _e(nc.sbuf_tensor("cinvn", [128, 1], F32))
        bq_sb = _e(nc.sbuf_tensor("bq_sb", [128, 24], F32))
        qbias = _e(nc.sbuf_tensor("qbias", [128, 16], F32))
        bo_sb = _e(nc.sbuf_tensor("bo_sb", [128, 16], F32))
        us_sb = _e(nc.sbuf_tensor("us_sb", [128, 64], F32))
        ds_sb = _e(nc.sbuf_tensor("ds_sb", [128, 16], F32))
        dsos = _e(nc.sbuf_tensor("dsos", [128, 16], F32))
        os_sb = _e(nc.sbuf_tensor("os_sb", [1, 1], F32))
        os_col = _e(nc.sbuf_tensor("os_col", [128, 1], F32))
        vb_row = _e(nc.sbuf_tensor("vb_row", [1, 512], F32))
        vb_bc = _e(nc.sbuf_tensor("vb_bc", [128, 512], BF16))
        recip_f = _e(nc.sbuf_tensor("recip_f", [128, 512], F32))
        bc_sb = _e(nc.sbuf_tensor("bc_sb", [128, 512], BF16))
        bc_sb2 = _e(nc.sbuf_tensor("bc_sb2", [128, 512], BF16))
        ao_tmp = _e(nc.sbuf_tensor("ao_tmp", [64, 1024], BF16))
        recip_b = _e(nc.sbuf_tensor("recip_b", [128, 512], BF16))
        onesb2 = _e(nc.sbuf_tensor("onesb2", [128, 128], BF16))
        A1 = _e(nc.sbuf_tensor("A1", [128, 16 * TOWN], BF16))
        Cxn = _e(nc.sbuf_tensor("C", [128, 16 * TOWN], BF16))
        W1 = _e(nc.sbuf_tensor("W1", [128, 8192], BF16))
        W2 = _e(nc.sbuf_tensor("W2", [128, 8192], BF16))
        Hb = _e(nc.sbuf_tensor("Hb", [128, 4096], BF16))
        sch.begin_phase()
        with nc.psum_tensor("p0a", [128, 1], F32) as p0a, \
             nc.psum_tensor("p0b", [128, 512], F32) as p0b, \
             nc.Block("blk_p0") as blk:
            def f(eng):
                eng.memset(ones_bf[:, :], 1.0)
                eng.memset(onesf[:, :], 1.0)
                eng.memset(onesb2[:, :], 1.0)
                eng.memset(c0125[:, :], 0.125)
                eng.memset(cn20[:, :], SHIFT)
                eng.memset(ceps[:, :], 1e-5)
                return eng.memset(cinvn[:, :], 1.0 / D)
            ev_ms = sch.add("v", f, inc=True)

            for dst, srcfn in (
                (bq_sb, lambda: bqkv.rearrange("(a b) -> b a", b=128)),
                (bo_sb, lambda: bo.rearrange("(a b) -> b a", b=128)),
                (us_sb, lambda: us.rearrange("(a b) -> b a", b=128)),
                (ds_sb, lambda: ds.rearrange("(a b) -> b a", b=128)),
                (os_sb, lambda: osc.rearrange("(a b) -> a b", a=1)),
                (vb_row, lambda: bqkv[2560:3072].rearrange("(a b) -> a b", a=1)),
            ):
                def f(eng, dst=dst, srcfn=srcfn):
                    return eng.dma_start(out=dst[:, :], in_=srcfn(),
                                         allow_slow_non_contiguous=True)
                ev_ld = sch.add("s", f, inc=True, inc_n=16)

            def f(eng):
                eng.matmul(p0a[:, :], onesf[0:1, :], os_sb[0:1, 0:1], start=True, stop=True)
                return eng.matmul(p0b[:, :], onesf[0:1, :], vb_row[0:1, :], start=True, stop=True)
            ev_bc = sch.add("t", f, waits=[ev_ms] + sch.dma_tokens("s"), inc=True)

            def f(eng):
                eng.tensor_copy(os_col[:, :], p0a[:, :])
                return eng.tensor_copy(vb_bc[:, :], p0b[:, :])
            ev_cp = sch.add("v", f, waits=[ev_bc], inc=True)

            def f(eng):
                eng.mul(dsos[:, :], ds_sb[:, :], os_col[:, 0:1])
                return eng.mul(qbias[:, :], bq_sb[:, 0:16], c0125[:, 0:1])
            sch.add("a", f, waits=[ev_cp] + sch.dma_tokens("s"), inc=True)
            sch.run(blk)

        with nc.sbuf_tensor("Bbuf", [128, 16 * TOWN], BF16) as Bbuf:
            with nc.sbuf_tensor("A2", [128, 4 * S + 16 * 520], BF16) as A2:
                Ksb = A2[:, 0:4 * S]
                Vaug_off = 4 * S
                if phases >= 1:
                    sch.begin_phase()
                    with nc.sbuf_tensor("rows", [1, 5 * 1024], F32) as rows, \
                         nc.sbuf_tensor("Ff", [128, 2048], F32) as Ff, \
                         nc.psum_tensor("p1s", [1, 1024], F32) as p1s, \
                         nc.psum_tensor("p1q", [1, 1024], F32) as p1q, \
                         nc.psum_tensor("p1b", [128, 512], F32) as p1b, \
                         nc.psum_tensor("p1c", [128, 512], F32) as p1c, \
                         nc.psum_tensor("p2a", [128, 512], F32) as p2a, \
                         nc.psum_tensor("p2b", [128, 512], F32) as p2b, \
                         nc.Block("blk_p12") as blk:
                        r_mean = rows[0:1, 0:1024]
                        r_ex2 = rows[0:1, 1024:2048]
                        r_var = r_ex2
                        r_rstd = rows[0:1, 2048:3072]
                        r_b = rows[0:1, 3072:4096]
                        r_tmp = rows[0:1, 4096:5120]
                        ev_sqmm = {}
                        ev_aff = {}
                        ev_xl = {}
                        ev_stats_prev = None
                        ev_bcv_h1 = None
                        for half, dstb in ((0, A1), (1, Bbuf)):
                            def f(eng, half=half, dstb=dstb):
                                return eng.dma_start(
                                    out=dstb.ap().rearrange("p (c t) -> p c t", t=TOWN),
                                    in_=xfull_r[:, :, half * TOWN:(half + 1) * TOWN])
                            ev_xl[half] = sch.add("g", f, inc=True, inc_n=16)
                        pre_toks = []
                        for sl in range(2):
                            Wb = W1 if sl % 2 == 0 else W2
                            def f(eng, sl=sl, Wb=Wb):
                                return eng.dma_start(
                                    out=Wb.ap().rearrange("p (c j) -> p c j", j=512),
                                    in_=wqkv_r[:, :, sl * 512:(sl + 1) * 512])
                            pre_toks.append(sch.add("g", f, inc=True, inc_n=16))
                        ev_vm = sch.add("v", lambda eng: eng.memset(A2[:, Vaug_off:], 1.0), inc=True)
                        for half in range(2):
                            xsrc = A1 if half == 0 else Bbuf
                            xn_dst = Cxn if half == 0 else Bbuf
                            hb_off = half * 2048
                            for fc in range(16):
                                gfc = half * 16 + fc
                                def f(eng, fc=fc, xsrc=xsrc):
                                    for n in range(2):
                                        ins = eng.matmul(
                                            p1s[0:1, n * 512:(n + 1) * 512], ones_bf[:, :],
                                            xsrc[:, fc * TOWN + n * 512: fc * TOWN + (n + 1) * 512],
                                            start=(fc == 0), stop=(fc == 15))
                                    return ins
                                sch.add("t", f, waits=[ev_xl[half], ev_stats_prev], inc=True)
                                def f(eng, fc=fc, xsrc=xsrc, hb_off=hb_off):
                                    return eng.activation(
                                        Hb[:, hb_off + (fc % 2) * 1024: hb_off + (fc % 2) * 1024 + 1024],
                                        xsrc[:, fc * TOWN:(fc + 1) * TOWN], AF.Square)
                                ev_sq = sch.add("a", f,
                                                waits=[ev_xl[half],
                                                       ev_sqmm.get(gfc - 2)],
                                                inc=True)
                                def f(eng, fc=fc, hb_off=hb_off):
                                    for n in range(2):
                                        ins = eng.matmul(
                                            p1q[0:1, n * 512:(n + 1) * 512], ones_bf[:, :],
                                            Hb[:, hb_off + (fc % 2) * 1024 + n * 512:
                                               hb_off + (fc % 2) * 1024 + (n + 1) * 512],
                                            start=(fc == 0), stop=(fc == 15))
                                    return ins
                                ev_sqmm[gfc] = sch.add("t", f,
                                                       waits=[ev_sq, ev_stats_prev],
                                                       inc=True)
                            def f(eng):
                                eng.tensor_scalar(r_mean, p1s[0:1, :], cinvn[0:1, 0:1], None, op0=ALU.mult)
                                eng.tensor_scalar(r_ex2, p1q[0:1, :], cinvn[0:1, 0:1], None, op0=ALU.mult)
                                eng.drain()
                                eng.tensor_tensor(r_tmp, r_mean, r_mean, op=ALU.mult)
                                eng.drain()
                                return eng.tensor_tensor(r_var, r_ex2, r_tmp, op=ALU.subtract)
                            ev_st = sch.add("v", f, waits=[ev_sqmm[half * 16 + 15]], inc=True)
                            ev_stats_prev = ev_st
                            def f(eng):
                                eng.activation(r_tmp, r_var, AF.Ln, bias=ceps[0:1, 0:1])
                                eng.drain()
                                return eng.activation(r_rstd, r_tmp, AF.Exp, scale=-0.5)
                            ev_rs = sch.add("a", f, waits=[ev_st], inc=True)
                            def f(eng):
                                return eng.scalar_tensor_tensor(
                                    r_b, r_mean, -1.0, r_rstd, op0=ALU.mult, op1=ALU.mult)
                            ev_bb = sch.add("v", f, waits=[ev_rs], inc=True)
                            ev_bcv = None
                            for n in range(2):
                                def f(eng, n=n):
                                    eng.matmul(p1b[:, :], onesf[0:1, :],
                                               r_rstd[0:1, n * 512:(n + 1) * 512],
                                               start=True, stop=True)
                                    return eng.matmul(p1c[:, :], onesf[0:1, :],
                                                      r_b[0:1, n * 512:(n + 1) * 512],
                                                      start=True, stop=True)
                                ev_bct = sch.add("t", f, waits=[ev_bb, ev_bcv], inc=True)
                                def f(eng, n=n):
                                    eng.tensor_copy(Ff[:, n * 512:(n + 1) * 512], p1b[:, :])
                                    return eng.tensor_copy(Ff[:, 1024 + n * 512:1024 + (n + 1) * 512],
                                                           p1c[:, :])
                                ev_bcv = sch.add("v", f, waits=[ev_bct], inc=True)
                            if half == 1:
                                ev_bcv_h1 = ev_bcv
                            for fc in range(16):
                                gfc = half * 16 + fc
                                def f(eng, fc=fc, xsrc=xsrc, xn_dst=xn_dst):
                                    if fc == 0:
                                        eng.drain()
                                    d = xn_dst[:, fc * TOWN:(fc + 1) * TOWN]
                                    eng.tensor_tensor(d, xsrc[:, fc * TOWN:(fc + 1) * TOWN],
                                                      Ff[:, 0:1024], op=ALU.mult)
                                    eng.drain()
                                    return eng.tensor_tensor(d, d, Ff[:, 1024:2048], op=ALU.add)
                                ev_aff[gfc] = sch.add("v", f, waits=[ev_bcv], inc=True)

                        if phases >= 2:
                            psq = [p2a, p2b]
                            psv = [p1b, p1c]
                            ev_mm_sl = {}
                            ev_mm_first = {}
                            ev_dma_sl = {}
                            ev_ev = {}
                            qv_i = 0
                            kv_i = 0
                            for sl in range(6):
                                Wb = W1 if sl % 2 == 0 else W2
                                if sl < 2:
                                    ev_dma_sl[sl] = pre_toks[sl]
                                else:
                                    def f(eng, sl=sl, Wb=Wb):
                                        return eng.dma_start(
                                            out=Wb.ap().rearrange("p (c j) -> p c j", j=512),
                                            in_=wqkv_r[:, :, sl * 512:(sl + 1) * 512])
                                    ev_dma_sl[sl] = sch.add("g", f,
                                                            waits=[ev_mm_sl.get(sl - 2),
                                                                   ev_mm_first.get(sl - 1)],
                                                            inc=True, inc_n=16)
                                if sl < 4:
                                    for fsub in range(4):
                                        qf = sl * 4 + fsub
                                        for tc in range(2):
                                            ps = psq[qv_i % 2]
                                            for c in range(16):
                                                def f(eng, Wb=Wb, fsub=fsub, tc=tc, ps=ps, c=c):
                                                    return eng.matmul(
                                                        ps[:, :],
                                                        Wb[:, c * 512 + fsub * 128: c * 512 + fsub * 128 + 128],
                                                        Cxn[:, c * TOWN + tc * 512: c * TOWN + tc * 512 + 512],
                                                        start=(c == 0), stop=(c == 15))
                                                ev_mm = sch.add(
                                                    "t", f,
                                                    waits=[ev_dma_sl[sl], ev_aff[c],
                                                           ev_ev.get(("q", qv_i % 2)) if c == 0 else None],
                                                    inc=True)
                                                if c == 0:
                                                    ev_mm_first.setdefault(sl, ev_mm)
                                            ev_mm_sl[sl] = ev_mm
                                            def f(eng, qf=qf, tc=tc, ps=ps):
                                                return eng.activation(
                                                    A1[:, qf * TOWN + tc * 512: qf * TOWN + tc * 512 + 512],
                                                    ps[:, :], AF.Identity,
                                                    bias=qbias[:, qf:qf + 1], scale=0.125)
                                            ev_ev[("q", qv_i % 2)] = sch.add("a", f, waits=[ev_mm], inc=True)
                                            qv_i += 1
                                elif sl == 4:
                                    for fsub in range(4):
                                        for tcf in range(4):
                                            ps = psq[qv_i % 2]
                                            src = Cxn if tcf < 2 else Bbuf
                                            tcl = tcf % 2
                                            hh = 0 if tcf < 2 else 1
                                            for c in range(16):
                                                def f(eng, Wb=Wb, fsub=fsub, src=src, tcl=tcl, ps=ps, c=c):
                                                    return eng.matmul(
                                                        ps[:, :],
                                                        Wb[:, c * 512 + fsub * 128: c * 512 + fsub * 128 + 128],
                                                        src[:, c * TOWN + tcl * 512: c * TOWN + tcl * 512 + 512],
                                                        start=(c == 0), stop=(c == 15))
                                                ev_mm = sch.add(
                                                    "t", f,
                                                    waits=[ev_dma_sl[sl], ev_aff[hh * 16 + c],
                                                           ev_ev.get(("q", qv_i % 2)) if c == 0 else None],
                                                    inc=True)
                                                if c == 0:
                                                    ev_mm_first.setdefault(sl, ev_mm)
                                            ev_mm_sl[sl] = ev_mm
                                            def f(eng, fsub=fsub, tcf=tcf, ps=ps):
                                                return eng.activation(
                                                    Ksb[:, fsub * S + tcf * 512: fsub * S + tcf * 512 + 512],
                                                    ps[:, :], AF.Identity,
                                                    bias=bq_sb[:, 16 + fsub:17 + fsub], scale=1.0)
                                            ev_ev[("q", qv_i % 2)] = sch.add("a", f, waits=[ev_mm], inc=True)
                                            qv_i += 1
                                else:
                                    for tt in range(16):
                                        ps = psv[kv_i % 2]
                                        src = Cxn if tt < 8 else Bbuf
                                        ttl = tt % 8
                                        hh = 0 if tt < 8 else 1
                                        for c in range(16):
                                            def f(eng, Wb=Wb, src=src, ttl=ttl, ps=ps, c=c):
                                                return eng.matmul(
                                                    ps[:, :],
                                                    src[:, c * TOWN + ttl * 128: c * TOWN + ttl * 128 + 128],
                                                    Wb[:, c * 512: c * 512 + 512],
                                                    start=(c == 0), stop=(c == 15))
                                            ev_mm = sch.add(
                                                "t", f,
                                                waits=[ev_dma_sl[sl], ev_aff[hh * 16 + c],
                                                       (ev_ev.get(("v", kv_i % 2)) if c == 0 else None),
                                                       (ev_vm if c == 0 else None),
                                                       (ev_bcv_h1 if c == 0 else None)],
                                                inc=True)
                                            if c == 0:
                                                ev_mm_first.setdefault(sl, ev_mm)
                                        ev_mm_sl[sl] = ev_mm
                                        def f(eng, tt=tt, ps=ps):
                                            o3 = A2[:, Vaug_off + tt * 520: Vaug_off + tt * 520 + 520]
                                            return eng.tensor_tensor(
                                                o3.rearrange("p (g e) -> p g e", e=65)[:, :, 0:64],
                                                ps[:, :].rearrange("p (g e) -> p g e", e=64),
                                                vb_bc[:, :].rearrange("p (g e) -> p g e", e=64),
                                                op=ALU.add)
                                        ev_ev[("v", kv_i % 2)] = sch.add("v", f, waits=[ev_mm], inc=True)
                                        kv_i += 1
                        sch.run(blk)
                    tap_block("tap_xn", Cxn.ap())
                    tap_block("tap_q", A1.ap())
                    tap_block("tap_k", Ksb)
                    tap_block("tap_vaug", A2[:, Vaug_off:])

                if phases >= 3:
                    sch.begin_phase()
                    with nc.psum_tensor("p3r0", [128, 1024], F32) as p3r0, \
                         nc.psum_tensor("p3r1", [128, 1024], F32) as p3r1, \
                         nc.psum_tensor("p3p0", [128, 512], F32) as p3p0, \
                         nc.psum_tensor("p3p1", [128, 512], F32) as p3p1, \
                         nc.psum_tensor("p3p2", [128, 512], F32) as p3p2, \
                         nc.psum_tensor("p3bc", [64, 512], F32) as p3bc, \
                         nc.Block("blk_p3") as blk:
                        regions = [p3r0, p3r1]
                        pvs = [p3p0, p3p1, p3p2]
                        expb = [W1, W2]
                        NU = p3_units
                        ev_exp_pair = {}
                        ev_exp_unit = {}
                        ev_pv = {}
                        ev_qk_pair = {}
                        ev_recip = {}
                        ev_mult = {}
                        ev_bct = {}
                        ev_aodma = {}
                        ev_bcc = {}

                        def unit_params(u):
                            h = u // 2
                            tc = u % 2
                            g = h // 4
                            c = g // 2
                            i = h % 4
                            b = (g % 2) * 64
                            return h, tc, g, c, i, b

                        for u in range(NU + 2):
                            if u < NU:
                                h, tc, g, c, i, b = unit_params(u)
                                qch = 4 * c + i
                                for j in range(8):
                                    p = 8 * u + j
                                    reg = regions[p % 2]
                                    def f(eng, j=j, reg=reg, b=b, c=c, qch=qch, tc=tc):
                                        for half in range(2):
                                            kt = 2 * j + half
                                            tp = (b, 0) if b else None
                                            ins = eng.matmul(
                                                reg[:, half * 512:(half + 1) * 512],
                                                Ksb[b:b + 64, c * S + kt * 128: c * S + kt * 128 + 128],
                                                A1[b:b + 64, qch * TOWN + tc * 512: qch * TOWN + tc * 512 + 512],
                                                start=True, stop=True, tile_position=tp)
                                        return ins
                                    ev_qk_pair[p] = sch.add(
                                        "t", f, waits=[ev_exp_pair.get(p - 2)], inc=True)
                                    def f(eng, u=u, j=j, reg=reg):
                                        return eng.activation(
                                            expb[u % 2][:, j * 1024:(j + 1) * 1024],
                                            reg[:, :], AF.Exp,
                                            bias=cn20[:, 0:1], scale=1.0)
                                    ev_exp_pair[p] = sch.add(
                                        "a", f, waits=[ev_qk_pair[p],
                                                       ev_pv.get(u - 2)], inc=True)
                                ev_exp_unit[u] = ev_exp_pair[8 * u + 7]
                            if u >= 1 and u - 1 < NU:
                                v_ = u - 1
                                hv, tcv, gv, cv, iv, bv = unit_params(v_)
                                def f(eng, v_=v_, gv=gv):
                                    eb = expb[v_ % 2]
                                    ps = pvs[v_ % 3]
                                    for kt in range(16):
                                        vbase = Vaug_off + kt * 520 + gv * 65
                                        ins = eng.matmul(
                                            ps[0:65, :],
                                            A2[:, vbase: vbase + 65],
                                            eb[:, kt * 512:(kt + 1) * 512],
                                            start=(kt == 0), stop=(kt == 15))
                                    return ins
                                ev_pv[v_] = sch.add(
                                    "t", f, waits=[ev_exp_unit.get(v_),
                                                   ev_mult.get(v_ - 3)], inc=True)
                            if u >= 2 and u - 2 < NU:
                                w_ = u - 2
                                hw, tcw, gw, cw, iw, bw = unit_params(w_)
                                def f(eng, w_=w_):
                                    return eng.matmul(p3bc[:, :],
                                                      onesb2[64:65, 0:64],
                                                      recip_b[64:65, :],
                                                      start=True, stop=True, tile_position=(64, 0))
                                ev_bct[w_] = sch.add(
                                    "t", f, waits=[ev_recip.get(w_),
                                                   ev_bcc.get(w_ - 1)], inc=True)

                            if u >= 1 and u - 1 < NU:
                                v_ = u - 1
                                def f(eng, v_=v_):
                                    eng.activation(recip_f[64:65, :],
                                                   pvs[v_ % 3][64:65, :], AF.Ln)
                                    eng.drain()
                                    return eng.activation(recip_b[64:65, :],
                                                          recip_f[64:65, :], AF.Exp,
                                                          scale=-1.0)
                                ev_recip[v_] = sch.add(
                                    "a", f, waits=[ev_pv[v_],
                                                   ev_bct.get(v_ - 1)], inc=True)
                            if u >= 2 and u - 2 < NU:
                                w_ = u - 2
                                hw, tcw, gw, cw, iw, bw = unit_params(w_)
                                qchw = 4 * cw + iw
                                if bw == 0:
                                    dst = Bbuf[0:64, qchw * TOWN + tcw * 512: qchw * TOWN + tcw * 512 + 512]
                                else:
                                    slot = w_ % 2
                                    dst = ao_tmp[0:64, slot * 512:(slot + 1) * 512]
                                def f(eng, w_=w_, dst=dst):
                                    bcb = [bc_sb, bc_sb2][w_ % 2]
                                    eng.tensor_copy(bcb[0:64, :], p3bc[:, :])
                                    eng.drain()
                                    return eng.tensor_tensor(
                                        dst, pvs[w_ % 3][0:64, :], bcb[0:64, :], op=ALU.mult)
                                ev_mult[w_] = sch.add(
                                    "v", f, waits=[ev_bct[w_],
                                                   ev_aodma.get(w_ - 2)], inc=True)
                                ev_bcc[w_] = ev_mult[w_]
                                if bw == 64:
                                    slot = w_ % 2
                                    def f(eng, w_=w_, qchw=qchw, tcw=tcw, slot=slot):
                                        return eng.dma_start(
                                            out=Bbuf[64:128, qchw * TOWN + tcw * 512: qchw * TOWN + tcw * 512 + 512],
                                            in_=ao_tmp[0:64, slot * 512:(slot + 1) * 512])
                                    ev_aodma[w_] = sch.add("s", f, waits=[ev_mult[w_]],
                                                           inc=True, inc_n=16)
                        def f(eng):
                            return eng.nop()
                        sch.add("s", f, waits=sch.dma_tokens("s"))
                        sch.run(blk)
                    tap_block("tap_exp", W1.ap())
                    tap_block("tap_ao", Bbuf.ap())

            if phases >= 4:
                sch.begin_phase()
                with nc.psum_tensor("p4a", [128, 512], F32) as p4a, \
                     nc.psum_tensor("p4b", [128, 512], F32) as p4b, \
                     nc.Block("blk_p4") as blk:
                    psq = [p4a, p4b]
                    ev_mm_sl = {}
                    ev_mm_first = {}
                    ev_ev = {}
                    qv_i = 0
                    for sl in range(4):
                        Wb = W1 if sl % 2 == 0 else W2
                        def f(eng, sl=sl, Wb=Wb):
                            return eng.dma_start(
                                out=Wb.ap().rearrange("p (c j) -> p c j", j=512),
                                in_=wo_r[:, :, sl * 512:(sl + 1) * 512])
                        ev_dma = sch.add("g", f, waits=[ev_mm_sl.get(sl - 2),
                                                        ev_mm_first.get(sl - 1)],
                                         inc=True, inc_n=16)
                        for fsub in range(4):
                            ef = sl * 4 + fsub
                            for tc in range(2):
                                ps = psq[qv_i % 2]
                                def f(eng, Wb=Wb, fsub=fsub, tc=tc, ps=ps):
                                    for c in range(16):
                                        ins = eng.matmul(
                                            ps[:, :],
                                            Wb[:, c * 512 + fsub * 128: c * 512 + fsub * 128 + 128],
                                            Bbuf[:, c * TOWN + tc * 512: c * TOWN + tc * 512 + 512],
                                            start=(c == 0), stop=(c == 15))
                                    return ins
                                ev_mm = sch.add("t", f,
                                                waits=[ev_dma,
                                                       ev_ev.get(qv_i % 2)],
                                                inc=True)
                                ev_mm_sl[sl] = ev_mm
                                ev_mm_first.setdefault(sl, ev_mm)
                                def f(eng, ef=ef, tc=tc, ps=ps):
                                    return eng.activation(
                                        A1[:, ef * TOWN + tc * 512: ef * TOWN + tc * 512 + 512],
                                        ps[:, :], AF.Identity, bias=bo_sb[:, ef:ef + 1], scale=1.0)
                                ev_ev[qv_i % 2] = sch.add("a", f, waits=[ev_mm], inc=True)
                                qv_i += 1
                    sch.run(blk)
                tap_block("tap_s", A1.ap())

        if phases >= 5:
            with nc.sbuf_tensor("Sacc", [128, 16 * TOWN], F32) as Sacc, \
                 nc.sbuf_tensor("W3", [128, 8192], BF16) as W3, \
                 nc.sbuf_tensor("xtile", [128, 2 * TOWN], F32) as xtile, \
                 nc.sbuf_tensor("gtmp", [128, 2048], BF16) as gtmp:
                sch.begin_phase()
                with nc.psum_tensor("p5u0", [128, 1024], F32) as p5u0, \
                     nc.psum_tensor("p5u1", [128, 1024], F32) as p5u1, \
                     nc.psum_tensor("p5d0", [128, 1024], F32) as p5d0, \
                     nc.psum_tensor("p5d1", [128, 1024], F32) as p5d1, \
                     nc.Block("blk_p5") as blk:
                    pu = [p5u0, p5u1]
                    pd = [p5d0, p5d1]
                    NHB = 16
                    ev_upsign = {}
                    ev_dnsign = {}
                    ev_updma = {}
                    ev_dndma = {}
                    ev_upmm = {}
                    ev_relu = {}
                    ev_dnmm = {}
                    ev_add = {}
                    for hb in range(NHB):
                        def f(eng, hb=hb):
                            return eng.dma_start(
                                out=W1.ap().rearrange("p (c j) -> p c j", j=512),
                                in_=upt_r[:, :, hb * 512:(hb + 1) * 512])
                        ev_updma[hb] = sch.add("g", f,
                                               waits=[ev_dnsign.get(hb - 1)],
                                               inc=True, inc_n=16)
                        def f(eng):
                            return eng.sign(W2[:, :], W1[:, :])
                        ev_upsign[hb] = sch.add("a", f,
                                                waits=[ev_updma[hb],
                                                       ev_upmm.get((hb - 1, 3))],
                                                inc=True)
                        def f(eng, hb=hb):
                            return eng.dma_start(
                                out=W1.ap().rearrange("p (c j) -> p c j", j=2048),
                                in_=dnt_r[:, hb * 4:(hb + 1) * 4, :])
                        ev_dndma[hb] = sch.add("g", f, waits=[ev_upsign[hb]],
                                               inc=True, inc_n=16)
                        def f(eng):
                            return eng.sign(W3[:, :], W1[:, :])
                        ev_dnsign[hb] = sch.add("a", f,
                                                waits=[ev_dndma[hb],
                                                       ev_dnmm.get((hb - 1, 15))],
                                                inc=True)
                        for hc in range(4):
                            ps = pu[hc % 2]
                            def f(eng, hb=hb, hc=hc, ps=ps):
                                for tc in range(2):
                                    for c in range(16):
                                        ins = eng.matmul(
                                            ps[:, tc * 512:(tc + 1) * 512],
                                            W2[:, c * 512 + hc * 128: c * 512 + hc * 128 + 128],
                                            Cxn[:, c * TOWN + tc * 512: c * TOWN + tc * 512 + 512],
                                            start=(c == 0), stop=(c == 15))
                                return ins
                            ev_upmm[(hb, hc)] = sch.add(
                                "t", f, waits=[ev_upsign[hb],
                                               ev_relu.get((hb, hc - 2),
                                                     ev_relu.get((hb - 1, hc + 2)))],
                                inc=True)
                            def f(eng, hb=hb, hc=hc, ps=ps):
                                return eng.activation(
                                    Hb[:, hc * 1024:(hc + 1) * 1024], ps[:, :], AF.Relu,
                                    scale=us_sb[:, hb * 4 + hc: hb * 4 + hc + 1])
                            ev_relu[(hb, hc)] = sch.add(
                                "a", f, waits=[ev_upmm[(hb, hc)],
                                               ev_dnmm.get((hb - 1, 15))], inc=True)
                        for df in range(16):
                            ps = pd[df % 2]
                            def f(eng, df=df, ps=ps):
                                for tc in range(2):
                                    for hc in range(4):
                                        ins = eng.matmul(
                                            ps[:, tc * 512:(tc + 1) * 512],
                                            W3[:, hc * 2048 + df * 128: hc * 2048 + df * 128 + 128],
                                            Hb[:, hc * 1024 + tc * 512: hc * 1024 + tc * 512 + 512],
                                            start=(hc == 0), stop=(hc == 3))
                                return ins
                            ev_dnmm[(hb, df)] = sch.add(
                                "t", f, waits=[ev_dnsign[hb],
                                               ev_relu[(hb, 3)],
                                               ev_add.get((hb, df - 2),
                                                     ev_add.get((hb - 1, df + 14)))],
                                inc=True)
                            def f(eng, hb=hb, df=df, ps=ps):
                                if hb == 0:
                                    return eng.tensor_copy(
                                        Sacc[:, df * TOWN:(df + 1) * TOWN], ps[:, :])
                                return eng.tensor_tensor(
                                    Sacc[:, df * TOWN:(df + 1) * TOWN],
                                    Sacc[:, df * TOWN:(df + 1) * TOWN], ps[:, :], op=ALU.add)
                            ev_add[(hb, df)] = sch.add(
                                "v", f, waits=[ev_dnmm[(hb, df)]], inc=True)
                    ev_silu = {}
                    ev_sg = {}
                    ev_xl6 = {}
                    ev_add6 = {}
                    ev_st6 = {}
                    for df in range(16):
                        def f(eng, df=df):
                            return eng.activation(
                                gtmp[:, (df % 2) * 1024:(df % 2) * 1024 + 1024],
                                Sacc[:, df * TOWN:(df + 1) * TOWN], AF.Silu,
                                scale=dsos[:, df:df + 1])
                        ev_silu[df] = sch.add("a", f,
                                              waits=[ev_add[(15, df)],
                                                     ev_add6.get(df - 2)], inc=True)
                        def f(eng, df=df):
                            return eng.dma_start(
                                out=xtile[:, (df % 2) * TOWN:(df % 2) * TOWN + TOWN],
                                in_=xfull[df * 128:(df + 1) * 128, 0:TOWN])
                        ev_xl6[df] = sch.add("s", f, waits=[ev_add6.get(df - 2)],
                                             inc=True, inc_n=16)
                        def f(eng, df=df):
                            eng.tensor_tensor(
                                gtmp[:, (df % 2) * 1024:(df % 2) * 1024 + 1024],
                                A1[:, df * TOWN:(df + 1) * TOWN],
                                gtmp[:, (df % 2) * 1024:(df % 2) * 1024 + 1024],
                                op=ALU.mult)
                            eng.drain()
                            return eng.tensor_tensor(
                                Sacc[:, df * TOWN:(df + 1) * TOWN],
                                xtile[:, (df % 2) * TOWN:(df % 2) * TOWN + TOWN],
                                gtmp[:, (df % 2) * 1024:(df % 2) * 1024 + 1024],
                                op=ALU.add)
                        ev_sg[df] = ev_add6[df] = sch.add(
                            "v", f, waits=[ev_silu[df], ev_xl6[df]], inc=True)
                        def f(eng, df=df):
                            return eng.dma_start(
                                out=out[df * 128:(df + 1) * 128, :],
                                in_=Sacc[:, df * TOWN:(df + 1) * TOWN])
                        ev_st6[df] = sch.add("s", f, waits=[ev_add6[df]],
                                             inc=True, inc_n=16)
                    def f(eng):
                        return eng.nop()
                    sch.add("s", f, waits=[ev_st6[15]] + sch.dma_tokens("s"))
                    sch.run(blk)

                if "tap_g" in tap_ts:
                    pass

    return nc



def _head_perm():
    perm = np.zeros(D, dtype=np.int64)
    for c in range(4):
        for i in range(4):
            for b in range(2):
                h = 8 * c + i + 4 * b
                newbase = (4 * c + i) * 128 + b * 64
                perm[newbase:newbase + 64] = h * 64 + np.arange(64)
    return perm


def _prep_inputs(L0_attn_x, L0_attn_Wqkv, L0_attn_bqkv, L0_attn_Wo, L0_attn_bo,
                 L0_ffn_up_weight, L0_ffn_down_weight, L0_ffn_up_scale,
                 L0_ffn_down_scale, L0_ffn_output_scale):
    perm = _head_perm()
    wq = np.asarray(L0_attn_Wqkv)[:, :D][:, perm]
    wkv = np.asarray(L0_attn_Wqkv)[:, D:]
    wqkv = np.ascontiguousarray(np.concatenate([wq, wkv], axis=1), dtype=np.float32)
    bq = np.asarray(L0_attn_bqkv)[:D][perm]
    bqkv = np.ascontiguousarray(np.concatenate([bq, np.asarray(L0_attn_bqkv)[D:]]),
                                dtype=np.float32)
    wo = np.ascontiguousarray(np.asarray(L0_attn_Wo)[perm, :], dtype=np.float32)
    upt = np.ascontiguousarray(np.asarray(L0_ffn_up_weight).T, dtype=np.float32)
    dnt = np.ascontiguousarray(np.asarray(L0_ffn_down_weight).T, dtype=np.float32)
    xT = np.ascontiguousarray(np.asarray(L0_attn_x).transpose(0, 2, 1), dtype=np.float32)

    common = {
        "wqkv": wqkv, "bqkv": bqkv, "wo": wo,
        "bo": np.ascontiguousarray(np.asarray(L0_attn_bo), dtype=np.float32),
        "upt": upt, "dnt": dnt,
        "us": np.ascontiguousarray(np.asarray(L0_ffn_up_scale), dtype=np.float32),
        "ds": np.ascontiguousarray(np.asarray(L0_ffn_down_scale), dtype=np.float32),
        "osc": np.ascontiguousarray(np.asarray(L0_ffn_output_scale), dtype=np.float32),
    }
    in_maps = []
    for core in range(8):
        b, half = core // 2, core % 2
        xb = xT[b]
        if half == 0:
            xc = xb
        else:
            xc = np.concatenate([xb[:, TOWN:], xb[:, :TOWN]], axis=1)
        m = dict(common)
        m["x"] = np.ascontiguousarray(xc)
        in_maps.append(m)
    return in_maps


def kernel(**inputs):
    in_maps = _prep_inputs(**inputs)
    nc = build_program()
    res = run_bass_kernel_spmd(nc, in_maps, core_ids=list(range(8)))
    outT = np.zeros((4, D, S), dtype=np.float32)
    for core in range(8):
        b, half = core // 2, core % 2
        outT[b][:, half * TOWN:(half + 1) * TOWN] = res.results[core]["out"]
    return np.ascontiguousarray(outT.transpose(0, 2, 1))
